# version 02 (final)
# baseline (speedup 1.0000x reference)
import numpy as np
import ml_dtypes
from contextlib import ExitStack
import concourse.bass as bass
import concourse.mybir as mybir
from concourse.bass_utils import run_bass_kernel_spmd

F32 = mybir.dt.float32
BF = mybir.dt.bfloat16
AF = mybir.ActivationFunctionType
OP = mybir.AluOpType
AX = mybir.AxisListType.X

T = 4096
D = 1024
ND = 8
NT = 32
NCH = 8
CH = 512
NEG = -1.0e30


class P:
    def __init__(self, nc):
        self.nc = nc
        self.seq = []

    def step(self, eng, fns, incs):
        self.seq.append((eng, fns, incs))

    def op(self, eng, meth, *a, **kw):
        self.step(eng, [lambda e, m=meth, a=a, kw=kw: getattr(e, m)(*a, **kw)], [1])

    def grp(self, eng, calls):
        fns = [(lambda e, m=m_, a=a_, kw=k_: getattr(e, m)(*a, **kw))
               for (m_, a_, k_) in calls]
        self.step(eng, fns, [1] * len(fns))

    def dma(self, out, in_):
        self.step('sync', [lambda e, o=out, i=in_: e.dma_start(out=o, in_=i)], [16])

    def dmag(self, pairs):
        fns = [(lambda e, o=o_, i=i_: e.dma_start(out=o, in_=i)) for (o_, i_) in pairs]
        self.step('sync', fns, [16] * len(fns))

    def mmg(self, calls):
        fns = [(lambda e, c=c: e.matmul(c[0], c[1], c[2], start=c[3], stop=c[4]))
               for c in calls]
        self.step('tensor', fns, [1] * len(fns))

    def mm(self, out, l, r, st=True, sp=True):
        self.mmg([(out, l, r, st, sp)])

    def tp(self, out, in_, ident):
        self.step('tensor', [lambda e, o=out, i=in_, d=ident: e.transpose(o, i, d)], [1])

    def cp(self, out, in_):
        self.op('scalar', 'copy', out, in_)

    def act(self, out, in_, f, **kw):
        self.op('scalar', 'activation', out, in_, f, **kw)

    def tt(self, out, a, b, o):
        self.op('vector', 'tensor_tensor', out, a, b, o)

    def ts(self, out, in0, s1, o0, s2=None, o1=None):
        if s2 is None:
            self.op('vector', 'tensor_scalar', out, in0, s1, None, o0)
        else:
            self.op('vector', 'tensor_scalar', out, in0, s1, s2, o0, o1)

    def stt(self, out, in0, s, in1, o0, o1):
        self.op('vector', 'scalar_tensor_tensor', out, in0, s, in1, o0, o1)

    def rmax(self, out, in_):
        self.op('vector', 'reduce_max', out, in_, AX)

    def rsum(self, out, in_):
        self.op('vector', 'reduce_sum', out, in_, AX)

    def recip(self, out, in_):
        self.op('vector', 'reciprocal', out, in_)

    def sq(self, out, in_):
        self.op('scalar', 'square', out, in_)

    def build(self, sem, block):
        total = sum(sum(i) for (_, _, i) in self.seq)
        def mk(name):
            def sec(e):
                t = 0
                for (en, fns, incs) in self.seq:
                    if en == name:
                        if t > 0:
                            e.wait_ge(sem, t)
                        for fn, inc in zip(fns, incs):
                            fn(e).then_inc(sem, inc)
                    t += sum(incs)
                if name == 'sync':
                    e.wait_ge(sem, total)
            return sec
        block.sync(mk('sync'))
        block.tensor(mk('tensor'))
        block.vector(mk('vector'))
        block.scalar(mk('scalar'))
        block.gpsimd(mk('gpsimd'))


def rearr_cols(dram_ap, c0, c1, p=128):
    return dram_ap[:, c0:c1].rearrange("(k p) m -> p k m", p=p)


def row_stats(p, srcs, nrows, pS1, pS2, ones_f, mrow, m2row, ssrow, vrow, sdrow, rrow, eps, inv_n):
    n = len(srcs)
    p.mmg([(pS1[0:1, 0:CH], ones_f[:, 0:1], srcs[i], i == 0, i == n - 1)
           for i in range(n)])
    if p._sqt is None:
        for i in range(n):
            p.sq(p._sq1[:, :], srcs[i])
            p.mm(pS2[0:1, 0:CH], ones_f[:, 0:1], p._sq1[:, :], i == 0, i == n - 1)
    else:
        for i in range(n):
            p.sq(p._sqt[i][:, :], srcs[i])
        p.mmg([(pS2[0:1, 0:CH], ones_f[:, 0:1], p._sqt[i][:, :], i == 0, i == n - 1)
               for i in range(n)])
    p.ts(mrow[0:1, :], pS1[0:1, 0:CH], inv_n, OP.mult)
    p.sq(m2row[0:1, :], mrow[0:1, :])
    p.ts(ssrow[0:1, :], pS2[0:1, 0:CH], inv_n, OP.mult)
    p.tt(ssrow[0:1, :], ssrow[0:1, :], m2row[0:1, :], OP.subtract)
    p.ts(ssrow[0:1, :], ssrow[0:1, :], eps, OP.add)
    p.act(ssrow[0:1, :], ssrow[0:1, :], AF.Sqrt)
    p.recip(rrow[0:1, :], ssrow[0:1, :])


def bcast2(p, ones_f, mrow, rrow, pB1, pB2, mB, rB):
    p.mm(pB1[:, 0:CH], ones_f[0:1, :], mrow[0:1, :])
    p.mm(pB2[:, 0:CH], ones_f[0:1, :], rrow[0:1, :])
    p.cp(mB[:, :], pB1[:, 0:CH])
    p.cp(rB[:, :], pB2[:, 0:CH])


def build_l1():
    nc = bass.Bass("TRN2", target_bir_lowering=False)
    x = nc.dram_tensor("x", [T, D], F32, kind="ExternalInput").ap()
    wq = nc.dram_tensor("wq", [D, D], F32, kind="ExternalInput").ap()
    wdkv = nc.dram_tensor("wdkv", [D, 256], F32, kind="ExternalInput").ap()
    wuk = nc.dram_tensor("wuk", [256, D], F32, kind="ExternalInput").ap()
    wuv = nc.dram_tensor("wuv", [256, D], F32, kind="ExternalInput").ap()
    wo = nc.dram_tensor("wo", [D, D], F32, kind="ExternalInput").ap()
    lg = nc.dram_tensor("lg", [128, 2], F32, kind="ExternalInput").ap()
    lb = nc.dram_tensor("lb", [128, 2], F32, kind="ExternalInput").ap()
    rc = nc.dram_tensor("rc", [64, T], F32, kind="ExternalInput").ap()
    rs = nc.dram_tensor("rs", [64, T], F32, kind="ExternalInput").ap()
    m4d = nc.dram_tensor("m4", [4, 128, CH], F32, kind="ExternalInput").ap()
    onesd = nc.dram_tensor("ones", [128, 128], F32, kind="ExternalInput").ap()
    identd = nc.dram_tensor("ident", [128, 128], F32, kind="ExternalInput").ap()
    y = nc.dram_tensor("y", [ND, 128, T], F32, kind="ExternalOutput").ap()
    oscr = nc.dram_tensor("oscr", [ND, 128, T], BF, kind="Internal").ap()

    es = ExitStack()
    with es:
        xh = [es.enter_context(nc.sbuf_tensor(f"xh{d}", [128, T], BF)) for d in range(ND)]
        latfc = [es.enter_context(nc.sbuf_tensor(f"latfc{i}", [128, CH], F32)) for i in range(2)]
        latb = [es.enter_context(nc.sbuf_tensor(f"latb{i}", [128, T], BF)) for i in range(2)]
        qkh = es.enter_context(nc.sbuf_tensor("qkh", [128, T], BF))
        vh = es.enter_context(nc.sbuf_tensor("vh", [128, NT * 64], BF))
        ET = es.enter_context(nc.sbuf_tensor("ET", [128, CH], BF))
        ones_f = es.enter_context(nc.sbuf_tensor("ones_f", [128, 128], F32))
        ones_b = es.enter_context(nc.sbuf_tensor("ones_b", [128, 128], BF))
        ident_f = es.enter_context(nc.sbuf_tensor("ident_f", [128, 128], F32))
        ident_b = es.enter_context(nc.sbuf_tensor("ident_b", [128, 128], BF))
        rcs = es.enter_context(nc.sbuf_tensor("rcs", [128, T], F32))
        m4s = [es.enter_context(nc.sbuf_tensor(f"m4_{i}", [128, CH], F32)) for i in range(4)]
        lg_s = es.enter_context(nc.sbuf_tensor("lg_s", [128, 2], F32))
        lb_s = es.enter_context(nc.sbuf_tensor("lb_s", [128, 2], F32))
        xtok = es.enter_context(nc.sbuf_tensor("xtok", [128, D], F32))
        xnb = es.enter_context(nc.sbuf_tensor("xnb", [128, D], BF))
        col = es.enter_context(nc.sbuf_tensor("col", [128, 8], F32))
        wtmp = es.enter_context(nc.sbuf_tensor("wtmp", [128, D], F32))
        wqh_b = es.enter_context(nc.sbuf_tensor("wqh_b", [128, CH], BF))
        wukh_b = es.enter_context(nc.sbuf_tensor("wukh_b", [128, 128], BF))
        wuvh_b = es.enter_context(nc.sbuf_tensor("wuvh_b", [128, 128], BF))
        wdkv_b = es.enter_context(nc.sbuf_tensor("wdkv_b", [128, 2 * D], BF))
        qsw = es.enter_context(nc.sbuf_tensor("qsw", [64, CH], F32))
        t1f = es.enter_context(nc.sbuf_tensor("t1f", [64, CH], F32))
        t2f = es.enter_context(nc.sbuf_tensor("t2f", [64, CH], F32))
        rsl = es.enter_context(nc.sbuf_tensor("rsl", [64, CH], F32))
        kstg = es.enter_context(nc.sbuf_tensor("kstg", [64, 128], BF))
        smf = es.enter_context(nc.sbuf_tensor("smf", [128, CH], F32))
        rrow = es.enter_context(nc.sbuf_tensor("rrow", [1, CH], F32))
        rB = es.enter_context(nc.sbuf_tensor("rB", [128, CH], F32))
        mB = es.enter_context(nc.sbuf_tensor("mB", [128, CH], F32))
        otmp = es.enter_context(nc.sbuf_tensor("otmp", [64, CH], BF))
        yc = [es.enter_context(nc.sbuf_tensor(f"yc{d}", [128, CH], F32)) for d in range(ND)]
        oc = [es.enter_context(nc.sbuf_tensor(f"oc{d}", [128, CH], BF)) for d in range(ND)]
        sq2 = [es.enter_context(nc.sbuf_tensor(f"sq2{d}", [128, CH], F32)) for d in range(ND)]
        wo1b = es.enter_context(nc.sbuf_tensor("wo1b", [128, D], BF))
        rows = es.enter_context(nc.sbuf_tensor("rows", [1, 4 * CH], F32))
        ocf = sq2
        pA = es.enter_context(nc.psum_tensor("pA", [128, CH], F32))
        pB = es.enter_context(nc.psum_tensor("pB", [128, CH], F32))
        pC = es.enter_context(nc.psum_tensor("pC", [128, CH], F32))
        pD = es.enter_context(nc.psum_tensor("pD", [128, CH], F32))
        pE = es.enter_context(nc.psum_tensor("pE", [128, CH], F32))
        pT = es.enter_context(nc.psum_tensor("pT", [128, 128], BF))
        sem = es.enter_context(nc.semaphore("gsem"))
        blk = es.enter_context(nc.Block())

        p = P(nc)
        p._sqt = sq2
        mrow = rows[0:1, 0:CH]; m2row = rows[0:1, CH:2 * CH]; ssrow = rows[0:1, 2 * CH:3 * CH]
        vrow = m2row; sdrow = ssrow; rr2 = rows[0:1, 3 * CH:4 * CH]

        p.dmag([(ones_f[:, :], onesd[:, :]), (rcs[0:64, :], rc[:, :]), (rcs[64:128, :], rs[:, :]),
                (lg_s[:, :], lg[:, :]), (lb_s[:, :], lb[:, :]),
                (ident_f[:, :], identd[:, :])] +
               [(m4s[i][:, :], m4d[i, :, :]) for i in range(4)])
        p.op('vector', 'tensor_copy', ones_b[:, :], ones_f[:, :])
        p.op('vector', 'tensor_copy', ident_b[:, :], ident_f[:, :])

        for tt_ in range(NT):
            p.dma(xtok[:, :], x[tt_ * 128:(tt_ + 1) * 128, :])
            p.rsum(col[:, 0:1], xtok[:, :])
            p.sq(sq2[0][:, :], xtok[:, 0:CH])
            p.sq(sq2[1][:, :], xtok[:, CH:D])
            p.rsum(col[:, 1:2], sq2[0][:, :])
            p.rsum(col[:, 4:5], sq2[1][:, :])
            p.tt(col[:, 1:2], col[:, 1:2], col[:, 4:5], OP.add)
            p.ts(col[:, 2:3], col[:, 0:1], 1.0 / D, OP.mult)
            p.sq(col[:, 3:4], col[:, 2:3])
            p.ts(col[:, 4:5], col[:, 1:2], 1.0 / D, OP.mult)
            p.tt(col[:, 5:6], col[:, 4:5], col[:, 3:4], OP.subtract)
            p.ts(col[:, 5:6], col[:, 5:6], 1e-5, OP.add)
            p.act(col[:, 6:7], col[:, 5:6], AF.Sqrt)
            p.recip(col[:, 7:8], col[:, 6:7])
            p.ts(col[:, 2:3], col[:, 2:3], -1.0, OP.mult)
            p.ts(xnb[:, :], xtok[:, :], col[:, 2:3], OP.add, col[:, 7:8], OP.mult)
            for d in range(ND):
                p.tp(pT[:, :], xnb[:, d * 128:(d + 1) * 128], ident_b[:, :])
                p.cp(xh[d][:, tt_ * 128:(tt_ + 1) * 128], pT[:, :])

        for m2 in range(2):
            p.dma(wtmp[:, 0:D].rearrange("p (k m) -> p k m", m=128),
                  rearr_cols(wdkv, m2 * 128, (m2 + 1) * 128))
            p.op('vector', 'tensor_copy', wdkv_b[:, m2 * D:(m2 + 1) * D], wtmp[:, 0:D])
        for c in range(NCH):
            cs = slice(c * CH, (c + 1) * CH)
            for m2 in range(2):
                p.mmg([(pA[:, :], wdkv_b[:, m2 * D + k * 128:m2 * D + (k + 1) * 128],
                        xh[k][:, cs], k == 0, k == 7) for k in range(ND)])
                p.cp(latfc[m2][:, :], pA[:, :])
            srcs = [latfc[m2][:, :] for m2 in range(2)]
            row_stats(p, srcs, 128, pC, pD, ones_f, mrow, m2row, ssrow, vrow, sdrow, rr2, 1e-5, 1.0 / 256)
            bcast2(p, ones_f, mrow, rr2, pA, pB, mB, rB)
            for m2 in range(2):
                p.tt(smf[:, :], latfc[m2][:, :], mB[:, :], OP.subtract)
                p.tt(yc[0][:, :], smf[:, :], rB[:, :], OP.mult)
                p.op('vector', 'tensor_scalar', latb[m2][:, cs], yc[0][:, :],
                     lg_s[:, m2:m2 + 1], lb_s[:, m2:m2 + 1], OP.mult, OP.add)

        for h in range(16):
            d2 = h // 2
            r0 = (h % 2) * 64
            p.dma(wtmp[:, 0:CH].rearrange("p (k m) -> p k m", m=64), rearr_cols(wq, h * 64, (h + 1) * 64))
            p.op('vector', 'tensor_copy', wqh_b[:, :], wtmp[:, 0:CH])
            p.dma(wtmp[:, 0:128].rearrange("p (k m) -> p k m", m=64), rearr_cols(wuk, h * 64, (h + 1) * 64))
            p.op('vector', 'tensor_copy', wukh_b[:, :], wtmp[:, 0:128])
            p.dma(wtmp[:, 0:128].rearrange("p (k m) -> p k m", m=64), rearr_cols(wuv, h * 64, (h + 1) * 64))
            p.op('vector', 'tensor_copy', wuvh_b[:, :], wtmp[:, 0:128])
            for c in range(NCH):
                cs = slice(c * CH, (c + 1) * CH)
                p.mmg([(pA[0:64, :], wqh_b[:, k * 64:(k + 1) * 64], xh[k][:, cs], k == 0, k == 7)
                       for k in range(ND)])
                p.grp('scalar', [('copy', (qsw[0:32, :], pA[32:64, 0:CH]), {}),
                                 ('copy', (qsw[32:64, :], pA[0:32, 0:CH]), {})])
                p.tt(t1f[:, :], pA[0:64, 0:CH], rcs[0:64, cs], OP.mult)
                p.cp(rsl[:, :], rcs[64:128, cs])
                p.tt(t2f[:, :], qsw[:, :], rsl[:, :], OP.mult)
                p.tt(t1f[:, :], t1f[:, :], t2f[:, :], OP.add)
                p.ts(qkh[0:64, cs], t1f[:, :], 0.125, OP.mult)
            for c in range(NCH):
                cs = slice(c * CH, (c + 1) * CH)
                p.mmg([(pA[0:64, :], wukh_b[:, m2 * 64:(m2 + 1) * 64], latb[m2][:, cs], m2 == 0, m2 == 1)
                       for m2 in range(2)])
                p.grp('scalar', [('copy', (qsw[0:32, :], pA[32:64, 0:CH]), {}),
                                 ('copy', (qsw[32:64, :], pA[0:32, 0:CH]), {})])
                p.tt(t1f[:, :], pA[0:64, 0:CH], rcs[0:64, cs], OP.mult)
                p.cp(rsl[:, :], rcs[64:128, cs])
                p.tt(t2f[:, :], qsw[:, :], rsl[:, :], OP.mult)
                p.tt(qkh[64:128, cs], t1f[:, :], t2f[:, :], OP.add)
            for tt_ in range(NT):
                p.mmg([(pB[:, 0:64], latb[m2][:, tt_ * 128:(tt_ + 1) * 128], wuvh_b[:, m2 * 64:(m2 + 1) * 64],
                        m2 == 0, m2 == 1) for m2 in range(2)])
                p.cp(vh[:, tt_ * 64:(tt_ + 1) * 64], pB[:, 0:64])
            for ic in range(NCH):
                jmax = 4 * (ic + 1)
                ics = slice(ic * CH, (ic + 1) * CH)
                for jb in range(jmax):
                    p.cp(kstg[:, :], qkh[64:128, jb * 128:(jb + 1) * 128])
                    p.mm(pA[:, :], kstg[:, :], qkh[0:64, ics])
                    ri = jb - 4 * ic
                    if ri >= 0:
                        p.tt(smf[:, :], pA[:, 0:CH], m4s[ri][:, :], OP.add)
                        p.act(ET[:, :], smf[:, :], AF.Exp)
                    else:
                        p.act(ET[:, :], pA[:, 0:CH], AF.Exp)
                    p.mmg([(pC[0:1, :], ones_b[:, 0:1], ET[:, :], jb == 0, jb == jmax - 1),
                           (pB[0:64, :], vh[:, jb * 64:(jb + 1) * 64], ET[:, :], jb == 0, jb == jmax - 1)])
                p.recip(rrow[0:1, :], pC[0:1, 0:CH])
                p.mm(pD[:, :], ones_f[0:1, :], rrow[0:1, :])
                p.cp(rB[:, :], pD[:, 0:CH])
                p.tt(otmp[:, :], pB[0:64, 0:CH], rB[0:64, :], OP.mult)
                p.dma(oscr[d2, r0:r0 + 64, ics], otmp[:, :])

        for c in range(NCH):
            cs = slice(c * CH, (c + 1) * CH)
            p.dmag([(oc[d][:, :], oscr[d, :, cs]) for d in range(ND)])
            for d in range(ND):
                p.dma(wtmp[:, :].rearrange("p (k m) -> p k m", m=128), rearr_cols(wo, d * 128, (d + 1) * 128))
                p.op('vector', 'tensor_copy', wo1b[:, :], wtmp[:, :])
                p.mmg([(pA[:, :], wo1b[:, k * 128:(k + 1) * 128], oc[k][:, :], k == 0, k == 7)
                       for k in range(ND)])
                p.cp(yc[d][:, :], pA[:, :])
            row_stats(p, [yc[d][:, :] for d in range(ND)], 128, pC, pD, ones_f,
                      mrow, m2row, ssrow, vrow, sdrow, rr2, 1e-5, 1.0 / D)
            bcast2(p, ones_f, mrow, rr2, pA, pB, mB, rB)
            for d in range(ND):
                p.tt(smf[:, :], yc[d][:, :], mB[:, :], OP.subtract)
                p.tt(ocf[d][:, :], smf[:, :], rB[:, :], OP.mult)
            for tl in range(4):
                p.dma(xtok[:, :], x[(c * 4 + tl) * 128:(c * 4 + tl + 1) * 128, :])
                for d in range(ND):
                    p.tp(pE[:, 0:128], xtok[:, d * 128:(d + 1) * 128], ident_f[:, :])
                    p.tt(ocf[d][:, tl * 128:(tl + 1) * 128], ocf[d][:, tl * 128:(tl + 1) * 128],
                         pE[:, 0:128], OP.add)
            p.dmag([(y[d, :, cs], ocf[d][:, :]) for d in range(ND)])

        p.build(sem, blk)
    return nc


def build_l2():
    nc = bass.Bass("TRN2", target_bir_lowering=False)
    xt = nc.dram_tensor("xt", [ND, 128, T], F32, kind="ExternalInput").ap()
    rwt = nc.dram_tensor("rwt", [D, 32], F32, kind="ExternalInput").ap()
    rbb = nc.dram_tensor("rbb", [128, 32], F32, kind="ExternalInput").ap()
    gateT = nc.dram_tensor("gateT", [D, 4096], F32, kind="ExternalInput").ap()
    upT = nc.dram_tensor("upT", [D, 4096], F32, kind="ExternalInput").ap()
    downM = nc.dram_tensor("downM", [4096, D], F32, kind="ExternalInput").ap()
    sgT = nc.dram_tensor("sgT", [D, 512], F32, kind="ExternalInput").ap()
    suT = nc.dram_tensor("suT", [D, 512], F32, kind="ExternalInput").ap()
    sdT = nc.dram_tensor("sdT", [512, D], F32, kind="ExternalInput").ap()
    selb = nc.dram_tensor("selb", [32, 4096], BF, kind="ExternalInput").ap()
    onesd = nc.dram_tensor("ones", [128, 128], F32, kind="ExternalInput").ap()
    identd = nc.dram_tensor("ident", [128, 128], F32, kind="ExternalInput").ap()
    y = nc.dram_tensor("y", [ND, 128, T], F32, kind="ExternalOutput").ap()
    yscr = nc.dram_tensor("yscr", [ND, 128, T], F32, kind="Internal").ap()

    es = ExitStack()
    with es:
        xh = [es.enter_context(nc.sbuf_tensor(f"xh{d}", [128, T], BF)) for d in range(ND)]
        hw = [es.enter_context(nc.sbuf_tensor(f"hw{e}", [128, CH], BF)) for e in range(32)]
        sh = [es.enter_context(nc.sbuf_tensor(f"sh{m}", [128, CH], BF)) for m in range(4)]
        sg_b = es.enter_context(nc.sbuf_tensor("sg_b", [128, 4096], BF))
        su_b = es.enter_context(nc.sbuf_tensor("su_b", [128, 4096], BF))
        sd_b = es.enter_context(nc.sbuf_tensor("sd_b", [128, 4096], BF))
        rwt_b = es.enter_context(nc.sbuf_tensor("rwt_b", [128, 256], BF))
        selb_s = es.enter_context(nc.sbuf_tensor("selb_s", [32, 4096], BF))
        rbb_s = es.enter_context(nc.sbuf_tensor("rbb_s", [128, 32], F32))
        wT32 = es.enter_context(nc.sbuf_tensor("wT32", [32, T], BF))
        ones_f = es.enter_context(nc.sbuf_tensor("ones_f", [128, 128], F32))
        ident_f = es.enter_context(nc.sbuf_tensor("ident_f", [128, 128], F32))
        wtmp = es.enter_context(nc.sbuf_tensor("wtmp", [128, D], F32))
        g_b = es.enter_context(nc.sbuf_tensor("g_b", [128, D], BF))
        u_b = es.enter_context(nc.sbuf_tensor("u_b", [128, D], BF))
        d_b = es.enter_context(nc.sbuf_tensor("d_b", [128, D], BF))
        xc = [es.enter_context(nc.sbuf_tensor(f"xc{d}", [128, CH], F32)) for d in range(ND)]
        sq2 = [es.enter_context(nc.sbuf_tensor(f"sq2{d}", [128, CH], F32)) for d in range(ND)]
        mB = es.enter_context(nc.sbuf_tensor("mB", [128, CH], F32))
        rB = es.enter_context(nc.sbuf_tensor("rB", [128, CH], F32))
        smf = es.enter_context(nc.sbuf_tensor("smf", [128, CH], F32))
        sif = es.enter_context(nc.sbuf_tensor("sif", [128, CH], F32))
        hw0 = es.enter_context(nc.sbuf_tensor("hw0s", [128, CH], F32))
        wBs = es.enter_context(nc.sbuf_tensor("wBs", [128, CH], F32))
        lgf = es.enter_context(nc.sbuf_tensor("lgf", [128, 32], F32))
        Ee = es.enter_context(nc.sbuf_tensor("Ee", [128, 32], F32))
        Ew = es.enter_context(nc.sbuf_tensor("Ew", [128, 32], F32))
        ind = es.enter_context(nc.sbuf_tensor("ind", [128, 32], F32))
        wtf = es.enter_context(nc.sbuf_tensor("wtf", [128, 32], F32))
        col = es.enter_context(nc.sbuf_tensor("col", [128, 8], F32))
        rows = es.enter_context(nc.sbuf_tensor("rows", [1, 4 * CH], F32))
        ps = [es.enter_context(nc.psum_tensor(f"ps{i}", [128, CH], F32)) for i in range(8)]
        sem = es.enter_context(nc.semaphore("gsem"))
        blk = es.enter_context(nc.Block())

        p = P(nc)
        p._sqt = sq2
        mrow = rows[0:1, 0:CH]; m2row = rows[0:1, CH:2 * CH]; ssrow = rows[0:1, 2 * CH:3 * CH]
        vrow = m2row; sdrow = ssrow; rr2 = rows[0:1, 3 * CH:4 * CH]

        p.dmag([(ones_f[:, :], onesd[:, :]), (ident_f[:, :], identd[:, :]),
                (selb_s[:, :], selb[:, :]), (rbb_s[:, :], rbb[:, :])])
        p.dma(wtmp[:, 0:256].rearrange("p (k m) -> p k m", m=32), rearr_cols(rwt, 0, 32))
        p.op('vector', 'tensor_copy', rwt_b[:, :], wtmp[:, 0:256])
        for j in range(4):
            p.dma(wtmp[:, :].rearrange("p (k m) -> p k m", m=128), rearr_cols(sgT, j * 128, (j + 1) * 128))
            p.op('vector', 'tensor_copy',
                 sg_b[:, :].rearrange("p (k m) -> p k m", m=512)[:, :, j * 128:(j + 1) * 128],
                 wtmp[:, :].rearrange("p (k m) -> p k m", m=128))
            p.dma(wtmp[:, :].rearrange("p (k m) -> p k m", m=128), rearr_cols(suT, j * 128, (j + 1) * 128))
            p.op('vector', 'tensor_copy',
                 su_b[:, :].rearrange("p (k m) -> p k m", m=512)[:, :, j * 128:(j + 1) * 128],
                 wtmp[:, :].rearrange("p (k m) -> p k m", m=128))
            p.dma(wtmp[:, :].rearrange("p (k m) -> p k m", m=256), rearr_cols(sdT, j * 256, (j + 1) * 256))
            p.op('vector', 'tensor_copy',
                 sd_b[:, :].rearrange("p (k m) -> p k m", m=1024)[:, :, j * 256:(j + 1) * 256],
                 wtmp[:, :].rearrange("p (k m) -> p k m", m=256))

        for c in range(NCH):
            cs = slice(c * CH, (c + 1) * CH)
            p.dmag([(xc[d][:, :], xt[d, :, cs]) for d in range(ND)])
            row_stats(p, [xc[d][:, :] for d in range(ND)], 128, ps[6], ps[7], ones_f,
                      mrow, m2row, ssrow, vrow, sdrow, rr2, 1e-5, 1.0 / D)
            bcast2(p, ones_f, mrow, rr2, ps[0], ps[1], mB, rB)
            for d in range(ND):
                p.tt(smf[:, :], xc[d][:, :], mB[:, :], OP.subtract)
                p.tt(xh[d][:, cs], smf[:, :], rB[:, :], OP.mult)

        for tt_ in range(NT):
            tsl = slice(tt_ * 128, (tt_ + 1) * 128)
            p.mmg([(ps[0][:, 0:32], xh[k][:, tsl], rwt_b[:, k * 32:(k + 1) * 32], k == 0, k == 7)
                   for k in range(ND)])
            p.tt(lgf[:, :], ps[0][:, 0:32], rbb_s[:, :], OP.add)
            p.rmax(col[:, 0:1], lgf[:, :])
            p.ts(col[:, 1:2], col[:, 0:1], -1.0, OP.mult)
            p.act(Ee[:, :], lgf[:, :], AF.Exp, bias=col[:, 1:2])
            p.op('vector', 'tensor_copy', Ew[:, :], Ee[:, :])
            for r in range(4):
                p.rmax(col[:, 2 + r:3 + r], Ew[:, :])
                if r < 3:
                    p.ts(ind[:, :], Ew[:, :], col[:, 2 + r:3 + r], OP.is_ge)
                    p.stt(Ew[:, :], ind[:, :], -1e30, Ew[:, :], OP.mult, OP.add)
            p.ts(ind[:, :], Ee[:, :], col[:, 5:6], OP.is_ge)
            p.tt(wtf[:, :], Ee[:, :], ind[:, :], OP.mult)
            p.rsum(col[:, 6:7], wtf[:, :])
            p.recip(col[:, 7:8], col[:, 6:7])
            p.ts(wtf[:, :], wtf[:, :], col[:, 7:8], OP.mult)
            p.tp(ps[4][0:32, 0:128], wtf[:, :], ident_f[:, :])
            p.cp(wT32[:, tsl], ps[4][0:32, 0:128])

        for c in range(NCH):
            cs = slice(c * CH, (c + 1) * CH)
            for m in range(4):
                p.mmg([(ps[0][:, :], sg_b[:, k * 512 + m * 128:k * 512 + (m + 1) * 128],
                        xh[k][:, cs], k == 0, k == 7) for k in range(ND)])
                p.mmg([(ps[1][:, :], su_b[:, k * 512 + m * 128:k * 512 + (m + 1) * 128],
                        xh[k][:, cs], k == 0, k == 7) for k in range(ND)])
                p.act(sif[:, :], ps[0][:, 0:CH], AF.Silu)
                p.tt(sh[m][:, :], sif[:, :], ps[1][:, 0:CH], OP.mult)
            for e in range(32):
                p.dma(wtmp[:, 0:D].rearrange("p (k m) -> p k m", m=128), rearr_cols(gateT, e * 128, (e + 1) * 128))
                p.op('vector', 'tensor_copy', g_b[:, :], wtmp[:, 0:D])
                p.dma(wtmp[:, 0:D].rearrange("p (k m) -> p k m", m=128), rearr_cols(upT, e * 128, (e + 1) * 128))
                p.op('vector', 'tensor_copy', u_b[:, :], wtmp[:, 0:D])
                p.mmg([(ps[0][:, :], g_b[:, k * 128:(k + 1) * 128], xh[k][:, cs], k == 0, k == 7)
                       for k in range(ND)])
                p.mmg([(ps[1][:, :], u_b[:, k * 128:(k + 1) * 128], xh[k][:, cs], k == 0, k == 7)
                       for k in range(ND)])
                p.act(sif[:, :], ps[0][:, 0:CH], AF.Silu)
                p.tt(hw0[:, :], sif[:, :], ps[1][:, 0:CH], OP.mult)
                p.mm(ps[2][:, :], selb_s[:, e * 128:(e + 1) * 128], wT32[:, cs])
                p.cp(wBs[:, :], ps[2][:, 0:CH])
                p.tt(hw[e][:, :], hw0[:, :], wBs[:, :], OP.mult)
            for e in range(32):
                p.dma(wtmp[:, 0:D], downM[e * 128:(e + 1) * 128, :])
                p.op('vector', 'tensor_copy', d_b[:, :], wtmp[:, 0:D])
                p.mmg([(ps[d][:, :], d_b[:, d * 128:(d + 1) * 128], hw[e][:, :], e == 0, False)
                       for d in range(ND)])
            for k4 in range(4):
                p.mmg([(ps[d][:, :], sd_b[:, k4 * 1024 + d * 128:k4 * 1024 + (d + 1) * 128],
                        sh[k4][:, :], False, k4 == 3) for d in range(ND)])
            for d in range(ND):
                p.cp(smf[:, :], ps[d][:, 0:CH])
                p.dma(yscr[d, :, cs], smf[:, :])

        for c in range(NCH):
            cs = slice(c * CH, (c + 1) * CH)
            p.dmag([(xc[d][:, :], yscr[d, :, cs]) for d in range(ND)])
            row_stats(p, [xc[d][:, :] for d in range(ND)], 128, ps[6], ps[7], ones_f,
                      mrow, m2row, ssrow, vrow, sdrow, rr2, 1e-5, 1.0 / D)
            bcast2(p, ones_f, mrow, rr2, ps[0], ps[1], mB, rB)
            for d in range(ND):
                p.tt(sif[:, :], xc[d][:, :], mB[:, :], OP.subtract)
                p.tt(sq2[d][:, :], sif[:, :], rB[:, :], OP.mult)
                p.dma(smf[:, :], xt[d, :, cs])
                p.tt(smf[:, :], smf[:, :], sq2[d][:, :], OP.add)
                p.dma(y[d, :, cs], smf[:, :])

        p.build(sem, blk)
    return nc


def build_l3():
    nc = bass.Bass("TRN2", target_bir_lowering=False)
    xt = nc.dram_tensor("xt", [ND, 128, T], F32, kind="ExternalInput").ap()
    wqT = nc.dram_tensor("wqT", [D, 24], F32, kind="ExternalInput").ap()
    wkT = nc.dram_tensor("wkT", [D, 24], F32, kind="ExternalInput").ap()
    wvT = nc.dram_tensor("wvT", [D, D], F32, kind="ExternalInput").ap()
    woT = nc.dram_tensor("woT", [D, D], F32, kind="ExternalInput").ap()
    scl = nc.dram_tensor("scl", [1, 1], F32, kind="ExternalInput").ap()
    pqd = nc.dram_tensor("pqd", [24, 32], BF, kind="ExternalInput").ap()
    m4d = nc.dram_tensor("m4", [4, 128, CH], F32, kind="ExternalInput").ap()
    onesd = nc.dram_tensor("ones", [128, 128], F32, kind="ExternalInput").ap()
    y = nc.dram_tensor("y", [ND, 128, T], F32, kind="ExternalOutput").ap()
    oscr = nc.dram_tensor("oscr", [ND, 128, T], BF, kind="Internal").ap()

    es = ExitStack()
    with es:
        xh = [es.enter_context(nc.sbuf_tensor(f"xh{d}", [128, T], BF)) for d in range(ND)]
        pk1 = es.enter_context(nc.sbuf_tensor("pk1", [24, T], BF))
        pk2 = es.enter_context(nc.sbuf_tensor("pk2", [24, T], BF))
        qa = es.enter_context(nc.sbuf_tensor("qa", [4, T], BF))
        ka = es.enter_context(nc.sbuf_tensor("ka", [4, T], BF))
        vh = es.enter_context(nc.sbuf_tensor("vh", [128, T], BF))
        ET = es.enter_context(nc.sbuf_tensor("ET", [128, CH], BF))
        ones_f = es.enter_context(nc.sbuf_tensor("ones_f", [128, 128], F32))
        ones_b = es.enter_context(nc.sbuf_tensor("ones_b", [128, 128], BF))
        m4s = [es.enter_context(nc.sbuf_tensor(f"m4_{i}", [128, CH], F32)) for i in range(4)]
        pq_s = es.enter_context(nc.sbuf_tensor("pq_s", [24, 32], BF))
        pqs_f = es.enter_context(nc.sbuf_tensor("pqs_f", [24, 4], BF))
        ssb = es.enter_context(nc.sbuf_tensor("ssb", [1, 1], F32))
        scol = es.enter_context(nc.sbuf_tensor("scol", [24, 1], F32))
        wtmp = es.enter_context(nc.sbuf_tensor("wtmp", [128, D], F32))
        wvh_b = es.enter_context(nc.sbuf_tensor("wvh_b", [128, D], BF))
        wo1b = es.enter_context(nc.sbuf_tensor("wo1b", [128, D], BF))
        wq_b = es.enter_context(nc.sbuf_tensor("wq_b", [128, 192], BF))
        wk_b = es.enter_context(nc.sbuf_tensor("wk_b", [128, 192], BF))
        xc = [es.enter_context(nc.sbuf_tensor(f"xc{d}", [128, CH], F32)) for d in range(ND)]
        sq2 = [es.enter_context(nc.sbuf_tensor(f"sq2{d}", [128, CH], F32)) for d in range(ND)]
        yc = [es.enter_context(nc.sbuf_tensor(f"yc{d}", [128, CH], F32)) for d in range(ND)]
        oc = [es.enter_context(nc.sbuf_tensor(f"oc{d}", [128, CH], BF)) for d in range(ND)]
        mB = es.enter_context(nc.sbuf_tensor("mB", [128, CH], F32))
        rB = es.enter_context(nc.sbuf_tensor("rB", [128, CH], F32))
        smf = es.enter_context(nc.sbuf_tensor("smf", [128, CH], F32))
        rrow = es.enter_context(nc.sbuf_tensor("rrow", [1, CH], F32))
        otmp = es.enter_context(nc.sbuf_tensor("otmp", [128, CH], BF))
        rows = es.enter_context(nc.sbuf_tensor("rows", [1, 4 * CH], F32))
        pA = es.enter_context(nc.psum_tensor("pA", [128, CH], F32))
        pB = es.enter_context(nc.psum_tensor("pB", [128, CH], F32))
        pC = es.enter_context(nc.psum_tensor("pC", [128, CH], F32))
        pD = es.enter_context(nc.psum_tensor("pD", [128, CH], F32))
        pE = es.enter_context(nc.psum_tensor("pE", [128, CH], F32))
        sem = es.enter_context(nc.semaphore("gsem"))
        blk = es.enter_context(nc.Block())

        p = P(nc)
        p._sqt = sq2
        mrow = rows[0:1, 0:CH]; m2row = rows[0:1, CH:2 * CH]; ssrow = rows[0:1, 2 * CH:3 * CH]
        vrow = m2row; sdrow = ssrow; rr2 = rows[0:1, 3 * CH:4 * CH]

        p.dmag([(ones_f[:, :], onesd[:, :]),
                (pq_s[:, :], pqd[:, :]), (ssb[:, :], scl[:, :])] +
               [(m4s[i][:, :], m4d[i, :, :]) for i in range(4)])
        p.op('vector', 'tensor_copy', ones_b[:, :], ones_f[:, :])
        p.dma(wtmp[:, 0:192].rearrange("p (k m) -> p k m", m=24), rearr_cols(wqT, 0, 24))
        p.op('vector', 'tensor_copy', wq_b[:, :], wtmp[:, 0:192])
        p.dma(wtmp[:, 0:192].rearrange("p (k m) -> p k m", m=24), rearr_cols(wkT, 0, 24))
        p.op('vector', 'tensor_copy', wk_b[:, :], wtmp[:, 0:192])

        for c in range(NCH):
            cs = slice(c * CH, (c + 1) * CH)
            p.dmag([(xc[d][:, :], xt[d, :, cs]) for d in range(ND)])
            row_stats(p, [xc[d][:, :] for d in range(ND)], 128, pC, pD, ones_f,
                      mrow, m2row, ssrow, vrow, sdrow, rr2, 1e-5, 1.0 / D)
            bcast2(p, ones_f, mrow, rr2, pA, pB, mB, rB)
            for d in range(ND):
                p.tt(smf[:, :], xc[d][:, :], mB[:, :], OP.subtract)
                p.tt(xh[d][:, cs], smf[:, :], rB[:, :], OP.mult)

        for c in range(NCH):
            cs = slice(c * CH, (c + 1) * CH)
            p.mmg([(pA[0:24, :], wq_b[:, k * 24:(k + 1) * 24], xh[k][:, cs], k == 0, k == 7)
                   for k in range(ND)])
            p.cp(pk1[:, cs], pA[0:24, 0:CH])
            p.mmg([(pB[0:24, :], wk_b[:, k * 24:(k + 1) * 24], xh[k][:, cs], k == 0, k == 7)
                   for k in range(ND)])
            p.cp(pk2[:, cs], pB[0:24, 0:CH])
        p.mm(pA[0:24, 0:1], ones_f[0:1, 0:24], ssb[0:1, 0:1])
        p.cp(scol[:, :], pA[0:24, 0:1])

        for h in range(8):
            p.op('vector', 'tensor_copy', pqs_f[:, :], pq_s[:, h * 4:(h + 1) * 4])
            p.ts(pqs_f[:, 0:3], pqs_f[:, 0:3], scol[:, 0:1], OP.mult)
            for c in range(NCH):
                cs = slice(c * CH, (c + 1) * CH)
                p.mm(pA[0:4, :], pqs_f[:, :], pk1[:, cs])
                p.cp(qa[:, cs], pA[0:4, 0:CH])
                p.mm(pB[0:4, :], pq_s[:, h * 4:(h + 1) * 4], pk2[:, cs])
                p.cp(ka[:, cs], pB[0:4, 0:CH])
            p.dma(wtmp[:, :].rearrange("p (k m) -> p k m", m=128), rearr_cols(wvT, h * 128, (h + 1) * 128))
            p.op('vector', 'tensor_copy', wvh_b[:, :], wtmp[:, :])
            for tt_ in range(NT):
                tsl = slice(tt_ * 128, (tt_ + 1) * 128)
                p.mmg([(pB[:, 0:128], xh[k][:, tsl], wvh_b[:, k * 128:(k + 1) * 128], k == 0, k == 7)
                       for k in range(ND)])
                p.cp(vh[:, tsl], pB[:, 0:128])
            for ic in range(NCH):
                jmax = 4 * (ic + 1)
                ics = slice(ic * CH, (ic + 1) * CH)
                for jb in range(jmax):
                    p.mm(pA[:, :], ka[:, jb * 128:(jb + 1) * 128], qa[:, ics])
                    ri = jb - 4 * ic
                    if ri >= 0:
                        p.tt(smf[:, :], pA[:, 0:CH], m4s[ri][:, :], OP.add)
                        p.act(ET[:, :], smf[:, :], AF.Exp)
                    else:
                        p.act(ET[:, :], pA[:, 0:CH], AF.Exp)
                    p.mmg([(pC[0:1, :], ones_b[:, 0:1], ET[:, :], jb == 0, jb == jmax - 1),
                           (pB[:, :], vh[:, jb * 128:(jb + 1) * 128], ET[:, :], jb == 0, jb == jmax - 1)])
                p.recip(rrow[0:1, :], pC[0:1, 0:CH])
                p.mm(pD[:, :], ones_f[0:1, :], rrow[0:1, :])
                p.cp(rB[:, :], pD[:, 0:CH])
                p.tt(otmp[:, :], pB[:, 0:CH], rB[:, :], OP.mult)
                p.dma(oscr[h, :, ics], otmp[:, :])

        for c in range(NCH):
            cs = slice(c * CH, (c + 1) * CH)
            p.dmag([(oc[d][:, :], oscr[d, :, cs]) for d in range(ND)] +
                   [(xc[d][:, :], xt[d, :, cs]) for d in range(ND)])
            for d in range(ND):
                p.dma(wtmp[:, :].rearrange("p (k m) -> p k m", m=128), rearr_cols(woT, d * 128, (d + 1) * 128))
                p.op('vector', 'tensor_copy', wo1b[:, :], wtmp[:, :])
                p.mmg([(pA[:, :], wo1b[:, k * 128:(k + 1) * 128], oc[k][:, :], k == 0, k == 7)
                       for k in range(ND)])
                p.cp(yc[d][:, :], pA[:, :])
            row_stats(p, [yc[d][:, :] for d in range(ND)], 128, pC, pD, ones_f,
                      mrow, m2row, ssrow, vrow, sdrow, rr2, 1e-5, 1.0 / D)
            bcast2(p, ones_f, mrow, rr2, pA, pB, mB, rB)
            for d in range(ND):
                p.tt(smf[:, :], yc[d][:, :], mB[:, :], OP.subtract)
                p.tt(sq2[d][:, :], smf[:, :], rB[:, :], OP.mult)
                p.tt(xc[d][:, :], xc[d][:, :], sq2[d][:, :], OP.add)
            p.dmag([(y[d, :, cs], xc[d][:, :]) for d in range(ND)])

        p.build(sem, blk)
    return nc


def build_l4():
    nc = bass.Bass("TRN2", target_bir_lowering=False)
    xt = nc.dram_tensor("xt", [ND, 128, T], F32, kind="ExternalInput").ap()
    wqkv = [nc.dram_tensor(f"wqkv{l}", [D, 1536], F32, kind="ExternalInput").ap() for l in range(2)]
    qs = [nc.dram_tensor(f"qs{l}", [64, 1], F32, kind="ExternalInput").ap() for l in range(2)]
    ks = [nc.dram_tensor(f"ks{l}", [64, 1], F32, kind="ExternalInput").ap() for l in range(2)]
    wol = [nc.dram_tensor(f"wo{l}", [512, D], F32, kind="ExternalInput").ap() for l in range(2)]
    lng = [nc.dram_tensor(f"lng{l}", [128, 8], F32, kind="ExternalInput").ap() for l in range(2)]
    lnb = [nc.dram_tensor(f"lnb{l}", [128, 8], F32, kind="ExternalInput").ap() for l in range(2)]
    w1a = [nc.dram_tensor(f"w1a{l}", [D, 2816], F32, kind="ExternalInput").ap() for l in range(2)]
    w1g = [nc.dram_tensor(f"w1g{l}", [D, 2816], F32, kind="ExternalInput").ap() for l in range(2)]
    b1a = [nc.dram_tensor(f"b1a{l}", [128, 22], F32, kind="ExternalInput").ap() for l in range(2)]
    b1g = [nc.dram_tensor(f"b1g{l}", [128, 22], F32, kind="ExternalInput").ap() for l in range(2)]
    w2 = [nc.dram_tensor(f"w2{l}", [2816, D], F32, kind="ExternalInput").ap() for l in range(2)]
    b2 = [nc.dram_tensor(f"b2{l}", [128, 8], F32, kind="ExternalInput").ap() for l in range(2)]
    cq = nc.dram_tensor("cq", [64, T], BF, kind="ExternalInput").ap()
    bq = nc.dram_tensor("bq", [64, T], BF, kind="ExternalInput").ap()
    ck = nc.dram_tensor("ck", [64, T], BF, kind="ExternalInput").ap()
    bk = nc.dram_tensor("bk", [64, T], BF, kind="ExternalInput").ap()
    nmp = nc.dram_tensor("nmp", [4, 128, CH], F32, kind="ExternalInput").ap()
    nmc = nc.dram_tensor("nmc", [4, 128, CH], F32, kind="ExternalInput").ap()
    onesd = nc.dram_tensor("ones", [128, 128], F32, kind="ExternalInput").ap()
    identd = nc.dram_tensor("ident", [128, 128], F32, kind="ExternalInput").ap()
    out = nc.dram_tensor("out", [T, D], F32, kind="ExternalOutput").ap()
    xf = nc.dram_tensor("xf", [ND, 128, T + 4], F32, kind="Internal").ap()
    oscr = nc.dram_tensor("oscr", [8, 64, T], BF, kind="Internal").ap()

    es = ExitStack()
    with es:
        hT = [es.enter_context(nc.sbuf_tensor(f"hT{d}", [128, T], BF)) for d in range(ND)]
        qkh = es.enter_context(nc.sbuf_tensor("qkh", [128, T], BF))
        vh = es.enter_context(nc.sbuf_tensor("vh", [128, NT * 64], BF))
        cbsA = es.enter_context(nc.sbuf_tensor("cbsA", [64, CH], BF))
        cbsB = es.enter_context(nc.sbuf_tensor("cbsB", [64, CH], BF))
        nmp_s = [es.enter_context(nc.sbuf_tensor(f"nmp{i}", [128, CH], F32)) for i in range(4)]
        nmc_s = [es.enter_context(nc.sbuf_tensor(f"nmc{i}", [128, CH], F32)) for i in range(4)]
        ones_f = es.enter_context(nc.sbuf_tensor("ones_f", [128, 128], F32))
        ident_f = es.enter_context(nc.sbuf_tensor("ident_f", [128, 128], F32))
        wqh_b = es.enter_context(nc.sbuf_tensor("wqh_b", [128, CH], BF))
        wkh_b = es.enter_context(nc.sbuf_tensor("wkh_b", [128, CH], BF))
        wvh_b = es.enter_context(nc.sbuf_tensor("wvh_b", [128, CH], BF))
        wosb = es.enter_context(nc.sbuf_tensor("wosb", [128, CH], BF))
        qs_s = es.enter_context(nc.sbuf_tensor("qs_s", [64, 2], F32))
        ks_s = es.enter_context(nc.sbuf_tensor("ks_s", [64, 2], F32))
        lng_s = es.enter_context(nc.sbuf_tensor("lng_s", [128, 16], F32))
        lnb_s = es.enter_context(nc.sbuf_tensor("lnb_s", [128, 16], F32))
        b1a_s = es.enter_context(nc.sbuf_tensor("b1a_s", [128, 44], F32))
        b1g_s = es.enter_context(nc.sbuf_tensor("b1g_s", [128, 44], F32))
        b2_s = es.enter_context(nc.sbuf_tensor("b2_s", [128, 16], F32))
        wtmp = es.enter_context(nc.sbuf_tensor("wtmp", [128, D], F32))
        qnb = es.enter_context(nc.sbuf_tensor("qnb", [64, CH], BF))
        qswb = es.enter_context(nc.sbuf_tensor("qswb", [64, CH], BF))
        kwc = es.enter_context(nc.sbuf_tensor("kwc", [64, CH], BF))
        kwp = es.enter_context(nc.sbuf_tensor("kwp", [64, CH], BF))
        wa_b = es.enter_context(nc.sbuf_tensor("wa_b", [128, D], BF))
        wg_b = es.enter_context(nc.sbuf_tensor("wg_b", [128, D], BF))
        wb2 = es.enter_context(nc.sbuf_tensor("wb2", [128, CH], BF))
        hh = es.enter_context(nc.sbuf_tensor("hh", [128, CH], BF))
        xfc = [es.enter_context(nc.sbuf_tensor(f"xfc{d}", [128, 516], F32)) for d in range(ND)]
        sqs = es.enter_context(nc.sbuf_tensor("sqs", [128, CH], F32))
        mB = es.enter_context(nc.sbuf_tensor("mB", [128, CH], F32))
        rB = es.enter_context(nc.sbuf_tensor("rB", [128, CH], F32))
        smf = es.enter_context(nc.sbuf_tensor("smf", [128, CH], F32))
        spf = es.enter_context(nc.sbuf_tensor("spf", [128, CH], F32))
        ecf = es.enter_context(nc.sbuf_tensor("ecf", [128, CH], F32))
        epf = es.enter_context(nc.sbuf_tensor("epf", [128, CH], F32))
        etb = es.enter_context(nc.sbuf_tensor("etb", [128, 128], BF))
        qnf4 = es.enter_context(nc.sbuf_tensor("qnf4", [64, CH], F32))
        t1f = es.enter_context(nc.sbuf_tensor("t1f4", [64, CH], F32))
        t2f = es.enter_context(nc.sbuf_tensor("t2f4", [64, CH], F32))
        sqf = es.enter_context(nc.sbuf_tensor("sqf4", [64, CH], F32))
        rb64 = es.enter_context(nc.sbuf_tensor("rb644", [64, CH], F32))
        qnf = qnf4[:, :]
        nrm = es.enter_context(nc.sbuf_tensor("nrm", [1, CH], F32))
        rn = es.enter_context(nc.sbuf_tensor("rn", [1, CH], F32))
        col = es.enter_context(nc.sbuf_tensor("col", [128, 8], F32))
        otmp = es.enter_context(nc.sbuf_tensor("otmp", [64, 128], BF))
        ggf = es.enter_context(nc.sbuf_tensor("ggf", [128, CH], F32))
        aaf = es.enter_context(nc.sbuf_tensor("aaf", [128, CH], F32))
        ytmp = es.enter_context(nc.sbuf_tensor("ytmp", [128, CH], F32))
        oc = [es.enter_context(nc.sbuf_tensor(f"oc{k}", [128, CH], BF)) for k in range(4)]
        rows = es.enter_context(nc.sbuf_tensor("rows", [1, 4 * CH], F32))
        zcol = es.enter_context(nc.sbuf_tensor("zcol", [128, 4], F32))
        ps = [es.enter_context(nc.psum_tensor(f"ps{i}", [128, CH], F32)) for i in range(7)]
        pE = es.enter_context(nc.psum_tensor("pE", [128, 128], F32))
        sem = es.enter_context(nc.semaphore("gsem"))
        blk = es.enter_context(nc.Block())

        p = P(nc)
        p._sqt = None
        p._sq1 = sqs
        mrow = rows[0:1, 0:CH]; m2row = rows[0:1, CH:2 * CH]; ssrow = rows[0:1, 2 * CH:3 * CH]
        vrow = m2row; sdrow = ssrow; rr2 = rows[0:1, 3 * CH:4 * CH]

        p.dmag([(ones_f[:, :], onesd[:, :]), (ident_f[:, :], identd[:, :])] +
               [(nmp_s[i][:, :], nmp[i, :, :]) for i in range(4)] +
               [(nmc_s[i][:, :], nmc[i, :, :]) for i in range(4)] +
               [(qs_s[:, l:l + 1], qs[l][:, :]) for l in range(2)] +
               [(ks_s[:, l:l + 1], ks[l][:, :]) for l in range(2)] +
               [(lng_s[:, l * 8:(l + 1) * 8], lng[l][:, :]) for l in range(2)] +
               [(lnb_s[:, l * 8:(l + 1) * 8], lnb[l][:, :]) for l in range(2)] +
               [(b1a_s[:, l * 22:(l + 1) * 22], b1a[l][:, :]) for l in range(2)] +
               [(b1g_s[:, l * 22:(l + 1) * 22], b1g[l][:, :]) for l in range(2)] +
               [(b2_s[:, l * 8:(l + 1) * 8], b2[l][:, :]) for l in range(2)])
        p.op('vector', 'memset', zcol[:, :], 0.0)
        p.dmag([(xf[d, :, 0:4], zcol[:, :]) for d in range(ND)])

        for c in range(NCH):
            cs = slice(c * CH, (c + 1) * CH)
            p.dmag([(xfc[d][:, 0:CH], xt[d, :, cs]) for d in range(ND)])
            row_stats(p, [xfc[d][:, 0:CH] for d in range(ND)], 128, ps[5], ps[6], ones_f,
                      mrow, m2row, ssrow, vrow, sdrow, rr2, 1e-5, 1.0 / D)
            bcast2(p, ones_f, mrow, rr2, ps[0], ps[1], mB, rB)
            for d in range(ND):
                p.tt(smf[:, :], xfc[d][:, 0:CH], mB[:, :], OP.subtract)
                p.tt(spf[:, :], smf[:, :], rB[:, :], OP.mult)
                p.dma(xf[d, :, 4 + c * CH:4 + (c + 1) * CH], spf[:, :])

        def load_shifted_hT(norm_l=None):
            if norm_l is None:
                for c in range(NCH):
                    p.dmag([(xfc[d][:, 0:513], xf[d, :, 3 + c * CH:3 + c * CH + 513]) for d in range(ND)])
                    for d in range(ND):
                        src = xfc[d][:, 0:CH] if d >= 4 else xfc[d][:, 1:513]
                        p.op('vector', 'tensor_copy', hT[d][:, c * CH:(c + 1) * CH], src)
            else:
                l = norm_l
                for c in range(NCH):
                    p.dmag([(xfc[d][:, 0:513], xf[d, :, 3 + c * CH:3 + c * CH + 513]) for d in range(ND)])
                    srcs = [xfc[d][:, 0:CH] if d >= 4 else xfc[d][:, 1:513] for d in range(ND)]
                    row_stats(p, srcs, 128, ps[5], ps[6], ones_f,
                              mrow, m2row, ssrow, vrow, sdrow, rr2, 1e-5, 1.0 / D)
                    bcast2(p, ones_f, mrow, rr2, ps[0], ps[1], mB, rB)
                    for d in range(ND):
                        p.tt(smf[:, :], srcs[d], mB[:, :], OP.subtract)
                        p.tt(spf[:, :], smf[:, :], rB[:, :], OP.mult)
                        p.op('vector', 'tensor_scalar', hT[d][:, c * CH:(c + 1) * CH], spf[:, :],
                             lng_s[:, l * 8 + d:l * 8 + d + 1], lnb_s[:, l * 8 + d:l * 8 + d + 1],
                             OP.mult, OP.add)

        for l in range(2):
            load_shifted_hT(None)
            for h in range(8):
                for (wb, off) in ((wqh_b, 0), (wkh_b, 512), (wvh_b, 1024)):
                    p.dma(wtmp[:, 0:CH].rearrange("p (k m) -> p k m", m=64),
                          rearr_cols(wqkv[l], off + h * 64, off + (h + 1) * 64))
                    p.op('vector', 'tensor_copy', wb[:, :], wtmp[:, 0:CH])
                for (ro, wb, ctd, btd, sc_col) in ((0, wqh_b, cq, bq, qs_s[:, l:l + 1]),
                                                   (64, wkh_b, ck, bk, ks_s[:, l:l + 1])):
                    for c in range(NCH):
                        cs = slice(c * CH, (c + 1) * CH)
                        p.mmg([(ps[4][0:64, :], wb[:, k * 64:(k + 1) * 64],
                                hT[k][:, cs], k == 0, k == 7) for k in range(ND)])
                        p.sq(sqf[:, :], ps[4][0:64, 0:CH])
                        p.mm(ps[3][0:1, :], ones_f[0:64, 0:1], sqf[:, :])
                        p.act(nrm[0:1, :], ps[3][0:1, 0:CH], AF.Sqrt)
                        p.recip(rn[0:1, :], nrm[0:1, :])
                        p.mm(ps[2][0:64, :], ones_f[0:1, 0:64], rn[0:1, :])
                        p.cp(rb64[:, :], ps[2][0:64, 0:CH])
                        p.tt(qnf[:, :], ps[4][0:64, 0:CH], rb64[:, :], OP.mult)
                        p.ts(qnf[:, :], qnf[:, :], sc_col, OP.mult)
                        p.grp('scalar', [('copy', (qnb[:, :], qnf[:, :]), {}),
                                         ('copy', (qswb[0:32, :], qnf[32:64, :]), {}),
                                         ('copy', (qswb[32:64, :], qnf[0:32, :]), {})])
                        p.dmag([(cbsA[:, :], ctd[:, cs]), (cbsB[:, :], btd[:, cs])])
                        p.tt(t1f[:, :], qnb[:, :], cbsA[:, :], OP.mult)
                        p.tt(t2f[:, :], qswb[:, :], cbsB[:, :], OP.mult)
                        p.tt(qkh[ro:ro + 64, cs], t1f[:, :], t2f[:, :], OP.add)
                for tt_ in range(NT):
                    tsl = slice(tt_ * 128, (tt_ + 1) * 128)
                    p.mmg([(ps[4][:, 0:64], hT[k][:, tsl], wvh_b[:, k * 64:(k + 1) * 64],
                            k == 0, k == 7) for k in range(ND)])
                    p.cp(vh[:, tt_ * 64:(tt_ + 1) * 64], ps[4][:, 0:64])
                for it in range(NT):
                    wi = it // 4
                    qb = it % 4
                    if qb == 0:
                        p.cp(kwc[:, :], qkh[64:128, wi * CH:(wi + 1) * CH])
                        if wi > 0:
                            p.cp(kwp[:, :], qkh[64:128, (wi - 1) * CH:wi * CH])
                    p.mm(ps[0][:, :], qkh[0:64, it * 128:(it + 1) * 128], kwc[:, :])
                    p.tt(smf[:, :], ps[0][:, 0:CH], nmc_s[qb][:, :], OP.add)
                    if wi > 0:
                        p.mm(ps[1][:, :], qkh[0:64, it * 128:(it + 1) * 128], kwp[:, :])
                        p.tt(spf[:, :], ps[1][:, 0:CH], nmp_s[qb][:, :], OP.add)
                    p.rmax(col[:, 0:1], smf[:, :])
                    if wi > 0:
                        p.rmax(col[:, 1:2], spf[:, :])
                        p.tt(col[:, 0:1], col[:, 0:1], col[:, 1:2], OP.max)
                    p.ts(col[:, 2:3], col[:, 0:1], -1.0, OP.mult)
                    p.act(ecf[:, :], smf[:, :], AF.Exp, bias=col[:, 2:3], accum_out=col[:, 3:4])
                    if wi > 0:
                        p.act(epf[:, :], spf[:, :], AF.Exp, bias=col[:, 2:3], accum_out=col[:, 4:5])
                        p.tt(col[:, 3:4], col[:, 3:4], col[:, 4:5], OP.add)
                    p.recip(col[:, 5:6], col[:, 3:4])
                    p.ts(ecf[:, :], ecf[:, :], col[:, 5:6], OP.mult)
                    if wi > 0:
                        p.ts(epf[:, :], epf[:, :], col[:, 5:6], OP.mult)
                    nj = 8 if wi > 0 else 4
                    first = True
                    for jb in range(nj):
                        if wi > 0:
                            jt = (wi - 1) * 4 + jb
                            src = epf if jb < 4 else ecf
                            jo = jb % 4
                        else:
                            jt = wi * 4 + jb
                            src = ecf
                            jo = jb
                        p.tp(pE[:, :], src[:, jo * 128:(jo + 1) * 128], ident_f[:, :])
                        p.cp(etb[:, :], pE[:, :])
                        p.mm(ps[2][0:64, 0:128], vh[:, jt * 64:(jt + 1) * 64], etb[:, :],
                             first, jb == nj - 1)
                        first = False
                    p.cp(otmp[:, :], ps[2][0:64, 0:128])
                    p.dma(oscr[h, :, it * 128:(it + 1) * 128], otmp[:, :])
            for c in range(NCH):
                cs = slice(c * CH, (c + 1) * CH)
                p.dmag([(oc[k][0:64, :], oscr[2 * k, :, cs]) for k in range(4)] +
                       [(oc[k][64:128, :], oscr[2 * k + 1, :, cs]) for k in range(4)] +
                       [(xfc[d][:, 0:CH], xf[d, :, 4 + c * CH:4 + (c + 1) * CH]) for d in range(ND)])
                for d in range(ND):
                    p.dma(wtmp[:, 0:CH].rearrange("p (k m) -> p k m", m=128),
                          rearr_cols(wol[l], d * 128, (d + 1) * 128, p=128))
                    p.op('vector', 'tensor_copy', wosb[:, :], wtmp[:, 0:CH])
                    p.mmg([(ps[0][:, :], wosb[:, k * 128:(k + 1) * 128], oc[k][:, :], k == 0, k == 3)
                           for k in range(4)])
                    p.tt(xfc[d][:, 0:CH], xfc[d][:, 0:CH], ps[0][:, 0:CH], OP.add)
                p.dmag([(xf[d, :, 4 + c * CH:4 + (c + 1) * CH], xfc[d][:, 0:CH]) for d in range(ND)])
            load_shifted_hT(norm_l=l)
            for c in range(NCH):
                cs = slice(c * CH, (c + 1) * CH)
                for half in range(2):
                    for m in range(22):
                        p.dma(wtmp[:, 0:D].rearrange("p (k m) -> p k m", m=128),
                              rearr_cols(w1a[l], m * 128, (m + 1) * 128))
                        p.op('vector', 'tensor_copy', wa_b[:, :], wtmp[:, 0:D])
                        p.dma(wtmp[:, 0:D].rearrange("p (k m) -> p k m", m=128),
                              rearr_cols(w1g[l], m * 128, (m + 1) * 128))
                        p.op('vector', 'tensor_copy', wg_b[:, :], wtmp[:, 0:D])
                        p.mmg([(ps[4][:, :], wa_b[:, k * 128:(k + 1) * 128], hT[k][:, cs], k == 0, k == 7)
                               for k in range(ND)])
                        p.mmg([(ps[5][:, :], wg_b[:, k * 128:(k + 1) * 128], hT[k][:, cs], k == 0, k == 7)
                               for k in range(ND)])
                        p.act(ggf[:, :], ps[5][:, 0:CH], AF.Gelu, bias=b1g_s[:, l * 22 + m:l * 22 + m + 1])
                        p.ts(aaf[:, :], ps[4][:, 0:CH], b1a_s[:, l * 22 + m:l * 22 + m + 1], OP.add)
                        p.tt(hh[:, :], aaf[:, :], ggf[:, :], OP.mult)
                        p.dma(wtmp[:, 0:CH], w2[l][m * 128:(m + 1) * 128, half * CH:(half + 1) * CH])
                        p.op('vector', 'tensor_copy', wb2[:, :], wtmp[:, 0:CH])
                        p.mmg([(ps[dd][:, :], wb2[:, dd * 128:(dd + 1) * 128], hh[:, :],
                                m == 0, m == 21) for dd in range(4)])
                    p.dmag([(xfc[half * 4 + dd][:, 0:CH], xf[half * 4 + dd, :, 4 + c * CH:4 + (c + 1) * CH])
                            for dd in range(4)])
                    for dd in range(4):
                        d = half * 4 + dd
                        p.ts(ytmp[:, :], ps[dd][:, 0:CH], b2_s[:, l * 8 + d:l * 8 + d + 1], OP.add)
                        p.tt(xfc[d][:, 0:CH], xfc[d][:, 0:CH], ytmp[:, :], OP.add)
                    p.dmag([(xf[half * 4 + dd, :, 4 + c * CH:4 + (c + 1) * CH], xfc[half * 4 + dd][:, 0:CH])
                            for dd in range(4)])

        xs = es.enter_context(nc.sbuf_tensor("xs", [128, 128], F32))
        xs2 = es.enter_context(nc.sbuf_tensor("xs2", [128, 128], F32))
        for c in range(NCH):
            cs = slice(c * CH, (c + 1) * CH)
            p.dmag([(xfc[d][:, 0:CH], xf[d, :, 4 + c * CH:4 + (c + 1) * CH]) for d in range(ND)])
            srcs = [xfc[d][:, 0:CH] for d in range(ND)]
            row_stats(p, srcs, 128, ps[5], ps[6], ones_f,
                      mrow, m2row, ssrow, vrow, sdrow, rr2, 1e-5, 1.0 / D)
            bcast2(p, ones_f, mrow, rr2, ps[0], ps[1], mB, rB)
            for tl in range(4):
                t128 = slice(tl * 128, (tl + 1) * 128)
                for d in range(ND):
                    p.tt(smf[:, 0:128], xfc[d][:, t128], mB[:, t128], OP.subtract)
                    p.tt(ecf[:, 0:128], smf[:, 0:128], rB[:, t128], OP.mult)
                    p.dma(xs[:, :], xt[d, :, c * CH + tl * 128:c * CH + (tl + 1) * 128])
                    p.tt(xs[:, :], xs[:, :], ecf[:, 0:128], OP.add)
                    p.tp(pE[:, :], xs[:, :], ident_f[:, :])
                    p.cp(xs2[:, :], pE[:, :])
                    p.dma(out[(c * 4 + tl) * 128:(c * 4 + tl + 1) * 128, d * 128:(d + 1) * 128], xs2[:, :])

        p.build(sem, blk)
    return nc


def _consts():
    c = {}
    c['ones'] = np.ones((128, 128), np.float32)
    c['ident'] = np.eye(128, dtype=np.float32)
    m4 = np.zeros((4, 128, CH), np.float32)
    for ri in range(4):
        pj = np.arange(128)[:, None]
        fi = np.arange(CH)[None, :]
        m4[ri] = np.where(ri * 128 + pj > fi, NEG, 0.0)
    c['m4'] = m4
    pos = np.arange(T, dtype=np.float64)
    invf = 1.0 / (10000.0 ** (np.arange(32, dtype=np.float64) / 32.0))
    ang = pos[None, :] * invf[:, None]
    cos = np.cos(ang); sin = np.sin(ang)
    C = np.concatenate([cos, cos], axis=0)
    S = np.concatenate([-sin, sin], axis=0)
    c['rc'] = C.astype(np.float32)
    c['rs'] = S.astype(np.float32)
    import itertools
    tri = np.array(list(itertools.product([-1.0, 1.0], repeat=3)), dtype=np.float64)
    tri = tri / np.linalg.norm(tri, axis=1, keepdims=True)
    dirs = tri[:8]
    pq = np.zeros((24, 32), np.float32)
    for h in range(8):
        for cc in range(3):
            pq[3 * h + cc, h * 4 + cc] = 1.0
            pq[3 * h + cc, h * 4 + 3] = dirs[h, cc]
    c['pqd'] = pq
    selb = np.zeros((32, 4096), np.float32)
    for e in range(32):
        selb[e, e * 128:(e + 1) * 128] = 1.0
    c['selb'] = selb
    DH = 64
    half_idx = np.arange(0, DH, 2, dtype=np.float64)
    invf2 = 1.0 / (10000.0 ** (half_idx / DH))
    t = np.arange(T, dtype=np.float64)
    fr = t[None, :] * invf2[:, None]
    fr = np.concatenate([fr, fr], axis=0)
    base = (half_idx + 0.4 * DH) / (1.4 * DH)
    power = (t - T // 2) / float(512 // 2)
    scale = base[:, None] ** power[None, :]
    scale = np.concatenate([scale, scale], axis=0)
    cosf = np.cos(fr); sinf = np.sin(fr)
    sgn = np.concatenate([-np.ones((32, T)), np.ones((32, T))], axis=0)
    c['cq'] = (cosf * scale * 8.0).astype(np.float32)
    c['bq'] = (sgn * sinf * scale * 8.0).astype(np.float32)
    c['ck'] = (cosf / scale).astype(np.float32)
    c['bk'] = (sgn * sinf / scale).astype(np.float32)
    nmc = np.zeros((4, 128, CH), np.float32)
    pi = np.arange(128)[:, None]
    jj = np.arange(CH)[None, :]
    for qb in range(4):
        nmc[qb] = np.where(jj > qb * 128 + pi, NEG, 0.0)
    c['nmp'] = m4
    c['nmc'] = nmc
    return c


def _xt_fmt(a):
    return np.ascontiguousarray(a.reshape(T, ND, 128).transpose(1, 2, 0))


def kernel(**inp):
    f = lambda k: np.ascontiguousarray(np.asarray(inp[k], np.float32))
    C = _consts()
    cores = list(range(8))
    x = f('L0_attn_x')

    lg = f('L0_attn_ln_gamma').reshape(2, 128).T.copy()
    lb = f('L0_attn_ln_beta').reshape(2, 128).T.copy()
    common1 = dict(wq=f('L0_attn_W_q'), wdkv=f('L0_attn_W_dkv'), wuk=f('L0_attn_W_uk'),
                   wuv=f('L0_attn_W_uv'), wo=f('L0_attn_W_o'), lg=lg, lb=lb,
                   rc=C['rc'], rs=C['rs'],
                   m4=C['m4'], ones=C['ones'], ident=C['ident'])
    nc1 = build_l1()
    maps = [dict(common1, x=np.ascontiguousarray(x[i])) for i in cores]
    r1 = run_bass_kernel_spmd(nc1, maps, core_ids=cores)
    ys = [r1.results[i]['y'] for i in range(8)]

    gw = f('L0_ffn_gate_w'); uw = f('L0_ffn_up_w'); dw = f('L0_ffn_down_w')
    common2 = dict(rwt=np.ascontiguousarray(f('L0_ffn_router_w').T),
                   rbb=np.ascontiguousarray(np.tile(f('L0_ffn_router_bias')[None, :], (128, 1))),
                   gateT=np.ascontiguousarray(gw.reshape(4096, D).T),
                   upT=np.ascontiguousarray(uw.reshape(4096, D).T),
                   downM=np.ascontiguousarray(dw.transpose(0, 2, 1).reshape(4096, D)),
                   sgT=np.ascontiguousarray(f('L0_ffn_sgate_w').T),
                   suT=np.ascontiguousarray(f('L0_ffn_sup_w').T),
                   sdT=np.ascontiguousarray(f('L0_ffn_sdown_w').T),
                   selb=np.ascontiguousarray(C['selb'].astype(ml_dtypes.bfloat16)),
                   ones=C['ones'], ident=C['ident'])
    nc2 = build_l2()
    maps = [dict(common2, xt=np.ascontiguousarray(ys[i])) for i in cores]
    r2 = run_bass_kernel_spmd(nc2, maps, core_ids=cores)
    ys = [r2.results[i]['y'] for i in range(8)]

    common3 = dict(wqT=np.ascontiguousarray(f('L1_attn_Wq').T),
                   wkT=np.ascontiguousarray(f('L1_attn_Wk').T),
                   wvT=np.ascontiguousarray(f('L1_attn_Wv').T),
                   woT=np.ascontiguousarray(f('L1_attn_Wo').T),
                   scl=np.asarray(inp['L1_attn_scale'], np.float32).reshape(1, 1),
                   pqd=np.ascontiguousarray(C['pqd'].astype(ml_dtypes.bfloat16)),
                   m4=C['m4'], ones=C['ones'])
    nc3 = build_l3()
    maps = [dict(common3, xt=np.ascontiguousarray(ys[i])) for i in cores]
    r3 = run_bass_kernel_spmd(nc3, maps, core_ids=cores)
    ys = [r3.results[i]['y'] for i in range(8)]

    wqkv = f('L1_ffn_wqkv'); qsc = f('L1_ffn_q_scale'); ksc = f('L1_ffn_k_scale')
    wo = f('L1_ffn_wo'); lng = f('L1_ffn_ln_g'); lnb = f('L1_ffn_ln_b')
    w1 = f('L1_ffn_w1'); b1 = f('L1_ffn_b1'); w2 = f('L1_ffn_w2'); b2 = f('L1_ffn_b2')
    HID = 2730
    b16 = lambda a: np.ascontiguousarray(a.astype(ml_dtypes.bfloat16))
    common4 = dict(cq=b16(C['cq']), bq=b16(C['bq']), ck=b16(C['ck']), bk=b16(C['bk']),
                   nmp=C['nmp'], nmc=C['nmc'], ones=C['ones'], ident=C['ident'])
    for l in range(2):
        w1a = np.zeros((D, 2816), np.float32); w1a[:, :HID] = w1[l][:, :HID]
        w1g = np.zeros((D, 2816), np.float32); w1g[:, :HID] = w1[l][:, HID:]
        w2p = np.zeros((2816, D), np.float32); w2p[:HID] = w2[l]
        b1ap = np.zeros(2816, np.float32); b1ap[:HID] = b1[l][:HID]
        b1gp = np.zeros(2816, np.float32); b1gp[:HID] = b1[l][HID:]
        common4.update({
            f'wqkv{l}': np.ascontiguousarray(wqkv[l]),
            f'qs{l}': np.ascontiguousarray(qsc[l].reshape(64, 1)),
            f'ks{l}': np.ascontiguousarray(ksc[l].reshape(64, 1)),
            f'wo{l}': np.ascontiguousarray(wo[l]),
            f'lng{l}': np.ascontiguousarray(lng[l].reshape(8, 128).T),
            f'lnb{l}': np.ascontiguousarray(lnb[l].reshape(8, 128).T),
            f'w1a{l}': w1a, f'w1g{l}': w1g, f'w2{l}': w2p,
            f'b1a{l}': np.ascontiguousarray(b1ap.reshape(22, 128).T),
            f'b1g{l}': np.ascontiguousarray(b1gp.reshape(22, 128).T),
            f'b2{l}': np.ascontiguousarray(b2[l].reshape(8, 128).T),
        })
    nc4 = build_l4()
    maps = [dict(common4, xt=np.ascontiguousarray(ys[i])) for i in cores]
    r4 = run_bass_kernel_spmd(nc4, maps, core_ids=cores)
    outs = [r4.results[i]['out'] for i in range(8)]
    return np.stack(outs, axis=0).astype(np.float32)
